# version 08
# baseline (speedup 1.0000x reference)
import sys
import numpy as np
import ml_dtypes
import concourse.bass as bass
import concourse.mybir as mybir
from concourse.bass_utils import run_bass_kernel_spmd

F32 = mybir.dt.float32
BF16 = mybir.dt.bfloat16
I32 = mybir.dt.int32
AX = mybir.AxisListType.X
OP = mybir.AluOpType
AF = mybir.ActivationFunctionType

B, T, D = 4, 4096, 1024
TC = 2048
NT = TC // 128
NC = D // 128
LN2F = float(np.float32(np.log(2.0)))
EPS = 1e-5
PHASE_LIMIT = 6
P4_SUB = 0
P3_SUB = 3
P3_NSL = 8
DEBUG_DUMP = None


def f32c(x):
    return float(np.float32(x))


def build_program():
    nc = bass.Bass("TRN2", target_bir_lowering=False)
    nc.detect_race_conditions = False

    x_d = nc.dram_tensor("x", [TC, D], F32, kind="ExternalInput").ap()
    xp_d = nc.dram_tensor("xp", [TC, D], F32, kind="ExternalInput").ap()
    wt_d = nc.dram_tensor("wt", [D, D], F32, kind="ExternalInput").ap()
    wf_d = nc.dram_tensor("wf", [D, D], BF16, kind="ExternalInput").ap()
    av_d = nc.dram_tensor("av", [128, NC], F32, kind="ExternalInput").ap()
    bv_d = nc.dram_tensor("bv", [128, NC], F32, kind="ExternalInput").ap()
    bo_d = nc.dram_tensor("bo", [1, D], F32, kind="ExternalInput").ap()
    bi_d = nc.dram_tensor("bi", [1, D], F32, kind="ExternalInput").ap()
    ws_d = nc.dram_tensor("wsr", [128, 1], F32, kind="ExternalInput").ap()
    hs_d = nc.dram_tensor("hs", [128, 1], F32, kind="ExternalInput").ap()
    id_d = nc.dram_tensor("ident", [128, 256], BF16, kind="ExternalInput").ap()
    out_d = nc.dram_tensor("out", [TC, D], F32, kind="ExternalOutput").ap()

    RW = NC * 1026
    names = ["ldA", "sqS", "smV", "apV", "trT", "evS", "scV", "rsS",
             "mmT", "s2S", "vtV", "r1V", "r4V", "st2", "gV", "gS"]

    import contextlib
    with contextlib.ExitStack() as stack:
        ec = stack.enter_context
        rhs = ec(nc.sbuf_tensor("rhs", [128, RW], BF16))
        yq = ec(nc.sbuf_tensor("yq", [128, NC * TC], BF16))
        x1 = ec(nc.sbuf_tensor("x1", [128, NT * 1024], F32))
        bufA = ec(nc.sbuf_tensor("bufA", [128, 8192], F32))
        bufB = ec(nc.sbuf_tensor("bufB", [128, 8192], BF16))
        bufC = ec(nc.sbuf_tensor("bufC", [128, 8192], BF16))
        sQ = ec(nc.sbuf_tensor("sQ", [128, 6144], BF16))
        ints = ec(nc.sbuf_tensor("ints", [128, 2048], I32))
        ahrep = ec(nc.sbuf_tensor("ahrep", [128, NC * 256], F32))
        ah8 = ec(nc.sbuf_tensor("ah8", [128, NC], F32))
        b8 = ec(nc.sbuf_tensor("b8", [128, NC], F32))
        ident = ec(nc.sbuf_tensor("identsb", [128, 256], BF16))
        onerow = ec(nc.sbuf_tensor("onerow", [1, 128], BF16))
        ivbf = ec(nc.sbuf_tensor("ivbf", [128, 16], BF16))
        borow = sQ[0:1, 0:1026]
        birow = sQ[0:1, 1152:2178]
        ivrow = bufB[0:1, 4096:6144]
        carry = ec(nc.sbuf_tensor("carry", [128, NC], F32))
        wscal = ec(nc.sbuf_tensor("wscal", [128, 1], F32))
        hsel = ec(nc.sbuf_tensor("hsel", [128, 1], F32))
        epsT = ec(nc.sbuf_tensor("epsT", [128, 1], F32))
        c211n = ec(nc.sbuf_tensor("c211n", [128, 1], F32))
        c211p = ec(nc.sbuf_tensor("c211p", [128, 1], F32))
        st = ec(nc.sbuf_tensor("stats", [128, 16 * 20], F32))
        wstage = bufA[:, 4096:5120]
        u1c = ec(nc.sbuf_tensor("u1c", [128, 1], F32))
        rowsum = ec(nc.sbuf_tensor("rowsum", [1, 4], F32))
        psT0 = ec(nc.psum_tensor("psT0", [128, 512], F32))
        psT1 = ec(nc.psum_tensor("psT1", [128, 512], F32))
        psA0 = ec(nc.psum_tensor("psA0", [128, 1024], F32))
        psA1 = ec(nc.psum_tensor("psA1", [128, 1024], F32))
        psS = ec(nc.psum_tensor("psS", [128, 32], F32))
        cms = [nc.semaphore(n) for n in names]
        sems = {n: cm.__enter__() for n, cm in zip(names, cms)}
        cnt = {n: 0 for n in names}
        psT = [psT0, psT1]
        psA = [psA0, psA1]

        def inc(ins, name, by=1):
            ins.then_inc(sems[name], by)
            cnt[name] += by

        def vg(vector, ins):
            inc(ins, "gV")
            vector.wait_ge(sems["gV"], cnt["gV"])

        def sg(scalar, ins):
            inc(ins, "gS")
            scalar.wait_ge(sems["gS"], cnt["gS"])

        def stc(i):
            return st[:, i * 16:(i + 1) * 16]

        (S_SX, S_SX2, S_M1, S_V1, S_R1, S_B1, S_TMP, S_TMP2,
         S_M2, S_V2, S_R2, S_B2, S_A2,
         S_SX1, S_SX12, S_M3, S_R3, S_B3, S_AM, S_AL) = range(20)

        PL = PHASE_LIMIT
        with nc.Block("p0", no_gpsimd_drain=True) as blk:
            ld0 = cnt["ldA"]
            gs0 = cnt["gS"]
            rs0 = cnt["rsS"]

            @blk.sync
            def _(sync):
                for dst, srcap in [(ident[:], id_d), (ah8[:], av_d), (b8[:], bv_d),
                                   (wscal[:], ws_d), (bufA[0:1, 0:1024], bo_d),
                                   (hsel[:], hs_d)]:
                    sync.wait_ge(sems["ldA"], 0)
                    sync.dma_start(out=dst, in_=srcap).then_inc(sems["ldA"], 16)
                for c in range(NC):
                    if c >= 1:
                        sync.wait_ge(sems["gS"], gs0 + 3 * (c - 1) + 1)
                    else:
                        sync.wait_ge(sems["ldA"], 0)
                    sync.dma_start(out=wstage,
                                   in_=wt_d[c * 128:(c + 1) * 128, :]).then_inc(sems["ldA"], 16)
            cnt["ldA"] += 16 * (6 + NC)
            ldv = ld0 + 16 * 6

            @blk.scalar
            def _(scalar):
                scalar.wait_ge(sems["ldA"], ldv)
                scalar.activation(borow[0:1, 0:1024], bufA[0:1, 0:1024], AF.Copy,
                                  bias=0.0, scale=1.0, accum_out=rowsum[0:1, 0:1])
                ins = scalar.activation(stc(S_TMP)[:, 0:NC], ah8[:], AF.Tanh, bias=0.0, scale=1.0)
                inc(ins, "rsS")
                for c in range(NC):
                    scalar.wait_ge(sems["ldA"], ldv + 16 * (c + 1))
                    ins = scalar.activation(rhs[:, c * 1026:c * 1026 + 1024], wstage,
                                            AF.Copy, bias=0.0, scale=b8[:, c:c + 1],
                                            accum_out=u1c[:, 0:1])
                    sg(scalar, ins)
                    ins = scalar.memzero(rhs[:, c * 1026 + 1024:c * 1026 + 1026])
                    sg(scalar, ins)
                    ins = scalar.copy(rhs[:, c * 1026 + 1024:c * 1026 + 1025], u1c[:, 0:1])
                    sg(scalar, ins)

            @blk.vector
            def _(vector):
                vector.memset(epsT[:], EPS)
                vector.memset(c211n[:], f32c(-211.0 * LN2F))
                vector.memset(c211p[:], f32c(211.0 * LN2F))
                vector.memset(onerow[:], 1.0)
                ins = vector.memset(ahrep[:], 0.0)
                vg(vector, ins)
                vector.wait_ge(sems["rsS"], rs0 + 1)
                for c in range(NC):
                    ins = vector.tensor_scalar(ahrep[:, c * 256:(c + 1) * 256],
                                               ahrep[:, c * 256:(c + 1) * 256],
                                               stc(S_TMP)[:, c:c + 1], None, OP.add)
                    vg(vector, ins)

        with nc.Block("p0b", no_gpsimd_drain=True) as blk:
            @blk.vector
            def _(vector):
                ins = vector.memset(borow[0:1, 1024:1026], 0.0)
                vg(vector, ins)
                vector.tensor_copy(borow[0:1, 1024:1025], rowsum[0:1, 0:1])

        def emit_lnscan(blkname, src_d, own):
            nsl = 4
            with nc.Block(blkname, no_gpsimd_drain=True) as blk:
                ld0 = cnt["ldA"]
                sq0 = cnt["sqS"]
                sm0 = cnt["smV"]
                ap0 = cnt["apV"]
                tr0 = cnt["trT"]
                ev0 = cnt["evS"]
                sc0 = cnt["scV"]
                rs0 = cnt["rsS"]
                vt0 = cnt["vtV"]

                def sq_t(k):
                    return sq0 + k + 1

                def ap_t(k):
                    return ap0 + k + 1

                def tr_t(s, c):
                    return tr0 + s * NC + c + 1

                def ev_t(s, c):
                    return ev0 + s * NC + c + 1

                def sc_t(s):
                    return sc0 + (s + 1) * NC

                @blk.sync
                def _(sync):
                    for s in range(nsl):
                        par = s % 2
                        for t in range(4):
                            k = s * 4 + t
                            if s >= 2:
                                sync.wait_ge(sems["apV"], ap_t((s - 2) * 4 + 3))
                                sync.wait_ge(sems["sqS"], sq_t((s - 2) * 4 + 3))
                            else:
                                sync.wait_ge(sems["ldA"], 0)
                            sync.dma_start(
                                out=bufA[:, par * 4096 + t * 1024: par * 4096 + (t + 1) * 1024],
                                in_=src_d[k * 128:(k + 1) * 128, :]).then_inc(sems["ldA"], 16)
                cnt["ldA"] += 16 * 16

                @blk.scalar
                def _(scalar):
                    for s in range(nsl):
                        par = s % 2
                        for t in range(4):
                            k = s * 4 + t
                            scalar.wait_ge(sems["ldA"], ld0 + 16 * (k + 1))
                            if s >= 2 and t == 0:
                                scalar.wait_ge(sems["trT"], tr_t(s - 2, NC - 1))
                            ins = scalar.activation(
                                bufB[:, par * 4096 + t * 1024: par * 4096 + (t + 1) * 1024],
                                bufA[:, par * 4096 + t * 1024: par * 4096 + (t + 1) * 1024],
                                AF.Square, bias=0.0, scale=1.0,
                                accum_out=stc(S_SX2)[:, k:k + 1])
                            inc(ins, "sqS")
                        scalar.wait_ge(sems["vtV"], vt0 + s + 1)
                        ins = scalar.activation(stc(S_TMP)[:, s * 4:s * 4 + 4],
                                                stc(S_V1)[:, s * 4:s * 4 + 4],
                                                AF.Ln, bias=epsT[:, 0:1], scale=1.0)
                        sg(scalar, ins)
                        ins = scalar.activation(stc(S_R1)[:, s * 4:s * 4 + 4],
                                                stc(S_TMP)[:, s * 4:s * 4 + 4],
                                                AF.Exp, bias=0.0, scale=-0.5)
                        inc(ins, "rsS")
                        for c in range(NC):
                            pe = (s * NC + c) % 2
                            scalar.wait_ge(sems["trT"], tr_t(s, c))
                            ins = scalar.activation(
                                bufC[:, par * 4096 + c * 512: par * 4096 + (c + 1) * 512],
                                psT[pe][:, 0:512], AF.Copy, bias=0.0, scale=1.0)
                            inc(ins, "evS")

                @blk.vector
                def _(vector):
                    for s in range(nsl):
                        par = s % 2
                        for t in range(4):
                            k = s * 4 + t
                            vector.wait_ge(sems["ldA"], ld0 + 16 * (k + 1))
                            vector.wait_ge(sems["sqS"], sq_t(k))
                            if s >= 2 and t == 0:
                                vector.wait_ge(sems["evS"], ev_t(s - 2, NC - 1))
                            ins = vector.tensor_scalar(
                                bufC[:, par * 4096 + t * 1024: par * 4096 + (t + 1) * 1024],
                                bufA[:, par * 4096 + t * 1024: par * 4096 + (t + 1) * 1024],
                                1.0, 0.0, OP.mult, OP.add,
                                accum_out=stc(S_SX)[:, k:k + 1])
                            inc(ins, "smV")
                        ins = vector.tensor_scalar(stc(S_M1)[:, s * 4:s * 4 + 4],
                                                   stc(S_SX)[:, s * 4:s * 4 + 4],
                                                   f32c(1.0 / 1024.0), None, OP.mult)
                        vg(vector, ins)
                        ins = vector.scalar_tensor_tensor(stc(S_TMP2)[:, s * 4:s * 4 + 4],
                                                          stc(S_M1)[:, s * 4:s * 4 + 4], 1.0,
                                                          stc(S_M1)[:, s * 4:s * 4 + 4],
                                                          OP.bypass, OP.mult)
                        vg(vector, ins)
                        ins = vector.scalar_tensor_tensor(stc(S_V1)[:, s * 4:s * 4 + 4],
                                                          stc(S_SX2)[:, s * 4:s * 4 + 4],
                                                          f32c(1.0 / 1024.0),
                                                          stc(S_TMP2)[:, s * 4:s * 4 + 4],
                                                          OP.mult, OP.subtract)
                        inc(ins, "vtV")
                        ins = vector.tensor_scalar(stc(S_B1)[:, s * 4:s * 4 + 4],
                                                   stc(S_M1)[:, s * 4:s * 4 + 4],
                                                   -1.0, None, OP.mult)
                        vg(vector, ins)
                        vector.wait_ge(sems["rsS"], rs0 + s + 1)
                        for t in range(4):
                            k = s * 4 + t
                            ins = vector.tensor_scalar(
                                bufB[:, par * 4096 + t * 1024: par * 4096 + (t + 1) * 1024],
                                bufA[:, par * 4096 + t * 1024: par * 4096 + (t + 1) * 1024],
                                stc(S_B1)[:, k:k + 1], stc(S_R1)[:, k:k + 1],
                                OP.add, OP.mult)
                            inc(ins, "apV")
                        if s >= 1:
                            vector.wait_ge(sems["scV"], sc_t(s - 1))
                        for c in range(NC):
                            vector.wait_ge(sems["evS"], ev_t(s, c))
                            if own:
                                outap = yq[:, c * TC + s * 512: c * TC + (s + 1) * 512]
                                ini = carry[:, c:c + 1] if s == 0 else \
                                    yq[:, c * TC + s * 512 - 1: c * TC + s * 512]
                            else:
                                outap = yq[:, par * 4096 + c * 512: par * 4096 + (c + 1) * 512]
                                ini = 0.0 if s == 0 else \
                                    yq[:, (1 - par) * 4096 + (c + 1) * 512 - 1:
                                       (1 - par) * 4096 + (c + 1) * 512]
                            in1ap = bufC[:, par * 4096 + c * 512: par * 4096 + (c + 1) * 512]
                            ins = vector.tensor_tensor_scan(
                                outap[:, 0:256], ahrep[:, c * 256:(c + 1) * 256],
                                in1ap[:, 0:256], ini, OP.mult, OP.add)
                            vg(vector, ins)
                            ins = vector.tensor_tensor_scan(
                                outap[:, 256:512], ahrep[:, c * 256:(c + 1) * 256],
                                in1ap[:, 256:512], outap[:, 255:256], OP.mult, OP.add)
                            inc(ins, "scV")
                    if not own:
                        vector.wait_ge(sems["scV"], sc_t(nsl - 1))
                        for c in range(NC):
                            ins = vector.tensor_scalar(
                                carry[:, c:c + 1],
                                yq[:, 4096 + (c + 1) * 512 - 1: 4096 + (c + 1) * 512],
                                hsel[:, 0:1], None, OP.mult)
                            vg(vector, ins)

                @blk.tensor
                def _(tensor):
                    for s in range(nsl):
                        par = s % 2
                        for c in range(NC):
                            pe = (s * NC + c) % 2
                            gidx = s * NC + c
                            if gidx >= 2:
                                pg = gidx - 2
                                tensor.wait_ge(sems["evS"], ev0 + pg + 1)
                            for t in range(4):
                                k = s * 4 + t
                                if c == 0:
                                    tensor.wait_ge(sems["apV"], ap_t(k))
                                ins = tensor.matmul(
                                    psT[pe][:, t * 128:(t + 1) * 128],
                                    bufB[:, par * 4096 + t * 1024 + c * 128:
                                         par * 4096 + t * 1024 + (c + 1) * 128],
                                    ident[:, 0:128], start=True, stop=True)
                                if t == 3:
                                    inc(ins, "trT")

        if PL >= 2:
            emit_lnscan("pF", xp_d, own=False)
        if PL >= 3:
            emit_lnscan("p1", x_d, own=True)

        if PL < 4:
            if DEBUG_DUMP == "yq":
                with nc.Block("dbg", no_gpsimd_drain=True) as blk:
                    @blk.vector
                    def _(vector):
                        for j in range(2):
                            ins = vector.tensor_copy(bufA[:, j * 4096:(j + 1) * 4096],
                                                     yq[:, j * 4096:(j + 1) * 4096])
                            inc(ins, "vtV")
                    @blk.sync
                    def _(sync):
                        sync.wait_ge(sems["vtV"], cnt["vtV"])
                        for rr in range(8):
                            sync.dma_start(out=out_d[rr * 128:(rr + 1) * 128, :],
                                           in_=bufA[:, rr * 1024:(rr + 1) * 1024]).then_inc(sems["st2"], 16)
                        sync.wait_ge(sems["st2"], cnt["st2"] + 128)
                cnt["st2"] += 128
            for cm in cms:
                cm.__exit__(None, None, None)
            return nc
        with nc.Block("p2", no_gpsimd_drain=True) as blk:
            ld0 = cnt["ldA"]
            mm0 = cnt["mmT"]
            s20 = cnt["s2S"]
            vt0 = cnt["vtV"]
            rs0 = cnt["rsS"]
            r10 = cnt["r1V"]
            sq0 = cnt["sqS"]

            @blk.sync
            def _(sync):
                for tau in range(NT):
                    par2 = (tau // 4) % 2
                    if tau >= 8:
                        sync.wait_ge(sems["r1V"], r10 + tau - 7)
                    else:
                        sync.wait_ge(sems["ldA"], 0)
                    sync.dma_start(out=bufA[:, par2 * 4096 + (tau % 4) * 1024:
                                            par2 * 4096 + (tau % 4 + 1) * 1024],
                                   in_=x_d[tau * 128:(tau + 1) * 128, :]).then_inc(sems["ldA"], 16)
            cnt["ldA"] += 16 * NT

            @blk.tensor
            def _(tensor):
                for tau in range(NT):
                    par = tau % 2
                    if tau >= 2:
                        tensor.wait_ge(sems["r1V"], r10 + tau - 1)
                        tensor.wait_ge(sems["s2S"], s20 + tau - 1)
                    for nh in range(2):
                        for c in range(NC):
                            tensor.matmul(psA[par][:, nh * 512:(nh + 1) * 512],
                                          yq[:, c * TC + tau * 128: c * TC + (tau + 1) * 128],
                                          rhs[:, c * 1026 + nh * 512: c * 1026 + (nh + 1) * 512],
                                          start=(c == 0), stop=False)
                        tensor.matmul(psA[par][:, nh * 512:(nh + 1) * 512],
                                      onerow[0:1, :], borow[0:1, nh * 512:(nh + 1) * 512],
                                      start=False, stop=True)
                    for c in range(NC):
                        tensor.matmul(psS[:, 2 * tau:2 * tau + 2],
                                      yq[:, c * TC + tau * 128: c * TC + (tau + 1) * 128],
                                      rhs[:, c * 1026 + 1024: c * 1026 + 1026],
                                      start=(c == 0), stop=False)
                    ins = tensor.matmul(psS[:, 2 * tau:2 * tau + 2],
                                        onerow[0:1, :], borow[0:1, 1024:1026],
                                        start=False, stop=True)
                    inc(ins, "mmT")

            @blk.scalar
            def _(scalar):
                for tau in range(NT):
                    par = tau % 2
                    scalar.wait_ge(sems["mmT"], mm0 + tau + 1)
                    ins = scalar.activation(bufB[:, 0:1024], psA[par][:, 0:1024],
                                            AF.Square, bias=0.0, scale=1.0,
                                            accum_out=stc(S_A2)[:, tau:tau + 1])
                    inc(ins, "s2S")
                    scalar.wait_ge(sems["vtV"], vt0 + tau + 1)
                    ins = scalar.activation(stc(S_TMP2)[:, tau:tau + 1],
                                            stc(S_V2)[:, tau:tau + 1],
                                            AF.Ln, bias=epsT[:, 0:1], scale=1.0)
                    sg(scalar, ins)
                    ins = scalar.activation(stc(S_R2)[:, tau:tau + 1],
                                            stc(S_TMP2)[:, tau:tau + 1],
                                            AF.Exp, bias=0.0, scale=-0.5)
                    inc(ins, "rsS")
                    scalar.wait_ge(sems["r1V"], r10 + tau + 1)
                    ins = scalar.activation(bufB[:, 1024:2048], x1[:, tau * 1024:(tau + 1) * 1024],
                                            AF.Square, bias=0.0, scale=1.0,
                                            accum_out=stc(S_SX12)[:, tau:tau + 1])
                    inc(ins, "sqS")

            @blk.vector
            def _(vector):
                for tau in range(NT):
                    par = tau % 2
                    par2 = (tau // 4) % 2
                    xoff = par2 * 4096 + (tau % 4) * 1024
                    vector.wait_ge(sems["mmT"], mm0 + tau + 1)
                    ins = vector.tensor_scalar(stc(S_M2)[:, tau:tau + 1],
                                               psS[:, 2 * tau:2 * tau + 1],
                                               f32c(1.0 / 1024.0), None, OP.mult)
                    vg(vector, ins)
                    ins = vector.scalar_tensor_tensor(stc(S_B2)[:, tau:tau + 1],
                                                      stc(S_M2)[:, tau:tau + 1], 1.0,
                                                      stc(S_M2)[:, tau:tau + 1],
                                                      OP.bypass, OP.mult)
                    vg(vector, ins)
                    vector.wait_ge(sems["s2S"], s20 + tau + 1)
                    ins = vector.scalar_tensor_tensor(stc(S_V2)[:, tau:tau + 1],
                                                      stc(S_A2)[:, tau:tau + 1],
                                                      f32c(1.0 / 1024.0),
                                                      stc(S_B2)[:, tau:tau + 1],
                                                      OP.mult, OP.subtract)
                    inc(ins, "vtV")
                    vector.wait_ge(sems["rsS"], rs0 + tau + 1)
                    ins = vector.scalar_tensor_tensor(stc(S_B2)[:, tau:tau + 1],
                                                      stc(S_M2)[:, tau:tau + 1], -1.0,
                                                      stc(S_R2)[:, tau:tau + 1],
                                                      OP.mult, OP.mult)
                    vg(vector, ins)
                    vector.wait_ge(sems["ldA"], ld0 + 16 * (tau + 1))
                    ins = vector.tensor_single_scalar(bufA[:, xoff:xoff + 1024],
                                                      bufA[:, xoff:xoff + 1024],
                                                      stc(S_B2)[:, tau:tau + 1], OP.add)
                    vg(vector, ins)
                    ins = vector.scalar_tensor_tensor(x1[:, tau * 1024:(tau + 1) * 1024],
                                                      psA[par][:, 0:1024],
                                                      stc(S_R2)[:, tau:tau + 1],
                                                      bufA[:, xoff:xoff + 1024],
                                                      OP.mult, OP.add,
                                                      accum_out=stc(S_SX1)[:, tau:tau + 1])
                    inc(ins, "r1V")

        if PL < 5:
            if DEBUG_DUMP is not None:
                with nc.Block("dbg", no_gpsimd_drain=True) as blk:
                    @blk.sync
                    def _(sync):
                        dsem = sems["st2"]
                        base = cnt["st2"]
                        if DEBUG_DUMP == "x1":
                            for tau in range(NT):
                                sync.dma_start(out=out_d[tau * 128:(tau + 1) * 128, :],
                                               in_=x1[:, tau * 1024:(tau + 1) * 1024]).then_inc(dsem, 16)
                            cnt["st2"] += 16 * NT
                        elif DEBUG_DUMP == "yq":
                            pass
                        sync.wait_ge(dsem, cnt["st2"])

            for cm in cms:
                cm.__exit__(None, None, None)
            return nc
        with nc.Block("p3", no_gpsimd_drain=True) as blk:
            ld0 = cnt["ldA"]
            tr0 = cnt["trT"]
            ev0 = cnt["evS"]
            vt0 = cnt["vtV"]
            rs0 = cnt["rsS"]
            qv0 = cnt["r4V"]
            gs0 = cnt["gS"]
            NSL = P3_NSL

            def rs_l2(q):
                return rs0 + 1 + 3 * q + 1

            def rs_sx(q):
                return rs0 + 1 + 3 * q + 3

            def vt_u2(q):
                return vt0 + 1 + 2 * q + 1

            def vt_kf(q):
                return vt0 + 1 + 2 * q + 2

            vt_iv = vt0 + 1 + (2 * NSL if P3_SUB >= 2 else 0) + 1

            def ev_q(q):
                return ev0 + (q + 1) * NC

            def tr_q(q):
                return tr0 + (q + 1) * NC

            def qv_t(q):
                return qv0 + q + 1

            @blk.sync
            def _(sync):
                for c in range(NC):
                    sync.wait_ge(sems["ldA"], 0)
                    sync.dma_start(out=rhs[:, c * 1026:c * 1026 + 1024],
                                   in_=wf_d[c * 128:(c + 1) * 128, :]).then_inc(sems["ldA"], 16)
            cnt["ldA"] += 16 * NC

            @blk.scalar
            def _(scalar):
                scalar.wait_ge(sems["vtV"], vt0 + 1)
                ins = scalar.activation(stc(S_TMP)[:], stc(S_TMP2)[:], AF.Ln,
                                        bias=epsT[:, 0:1], scale=1.0)
                sg(scalar, ins)
                ins = scalar.activation(stc(S_R3)[:], stc(S_TMP)[:], AF.Exp,
                                        bias=0.0, scale=-0.5)
                inc(ins, "rsS")
                for c in range(NC):
                    scalar.wait_ge(sems["ldA"], ld0 + 16 * (c + 1))
                    ins = scalar.activation(wstage, rhs[:, c * 1026:c * 1026 + 1024],
                                            AF.Copy, bias=0.0, scale=1.0,
                                            accum_out=u1c[:, 0:1])
                    sg(scalar, ins)
                    ins = scalar.memzero(rhs[:, c * 1026 + 1024:c * 1026 + 1026])
                    sg(scalar, ins)
                    ins = scalar.copy(rhs[:, c * 1026 + 1024:c * 1026 + 1025], u1c[:, 0:1])
                    sg(scalar, ins)
                def emit_evacs(qq):
                    if P3_SUB < 3:
                        return
                    for c in range(NC):
                        r = c % 4
                        scalar.wait_ge(sems["trT"], tr0 + qq * NC + c + 1)
                        ins = scalar.activation(
                            yq[:, c * TC + qq * 256: c * TC + qq * 256 + 256],
                            psT[r // 2][:, (r % 2) * 256:(r % 2) * 256 + 256],
                            AF.Copy, bias=0.0, scale=1.0)
                        inc(ins, "evS")

                A0 = bufA[:, 0:2048]
                for q in range(NSL if P3_SUB >= 2 else 0):
                    pq = q % 2
                    C0 = bufC[:, pq * 4096 + 0: pq * 4096 + 2048]
                    C1 = bufC[:, pq * 4096 + 2048: pq * 4096 + 4096]
                    S0 = sQ[:, pq * 2048:(pq + 1) * 2048]
                    if q >= 1:
                        emit_evacs(q - 1)
                    scalar.wait_ge(sems["vtV"], vt_u2(q))
                    ins = scalar.activation(C0, A0, AF.Ln, bias=0.0, scale=65536.0)
                    inc(ins, "rsS")
                    scalar.wait_ge(sems["vtV"], vt_kf(q))
                    ins = scalar.activation(S0, C0, AF.Exp, bias=c211n[:, 0:1], scale=LN2F)
                    inc(ins, "rsS")
                    ins = scalar.activation(C1, C0, AF.Exp, bias=c211p[:, 0:1], scale=-LN2F)
                    inc(ins, "rsS")
                emit_evacs(NSL - 1)
                for g in range(4 if P3_SUB == 3 else 0):
                    scalar.wait_ge(sems["trT"], tr0 + NSL * NC + g + 1)
                    ins = scalar.activation(ivrow[0:1, g * 512:(g + 1) * 512],
                                            psT[g % 2][0:1, 0:512],
                                            AF.Copy, bias=0.0, scale=1.0)
                    inc(ins, "evS")

            @blk.vector
            def _(vector):
                ins = vector.tensor_scalar(stc(S_M3)[:], stc(S_SX1)[:],
                                           f32c(1.0 / 1024.0), None, OP.mult)
                vg(vector, ins)
                ins = vector.scalar_tensor_tensor(stc(S_TMP)[:], stc(S_M3)[:], 1.0,
                                                  stc(S_M3)[:], OP.bypass, OP.mult)
                vg(vector, ins)
                ins = vector.scalar_tensor_tensor(stc(S_TMP2)[:], stc(S_SX12)[:],
                                                  f32c(1.0 / 1024.0), stc(S_TMP)[:],
                                                  OP.mult, OP.subtract)
                inc(ins, "vtV")
                vector.wait_ge(sems["rsS"], rs0 + 1)
                ins = vector.scalar_tensor_tensor(stc(S_B3)[:], stc(S_M3)[:], -1.0,
                                                  stc(S_R3)[:], OP.mult, OP.mult)
                vg(vector, ins)
                A0 = bufA[:, 0:2048]
                A1 = bufA[:, 2048:4096]
                A2 = bufA[:, 5120:7168]
                for q in range(NSL if P3_SUB >= 2 else 0):
                    pq = q % 2
                    C0 = bufC[:, pq * 4096 + 0: pq * 4096 + 2048]
                    C1 = bufC[:, pq * 4096 + 2048: pq * 4096 + 4096]
                    S0 = sQ[:, pq * 2048:(pq + 1) * 2048]
                    E0 = sQ[:, 4096:6144]
                    if q >= 2 and P3_SUB >= 3:
                        vector.wait_ge(sems["trT"], tr_q(q - 2))
                    for j in range(2):
                        tau = 2 * q + j
                        ins = vector.tensor_scalar(
                            A0[:, j * 1024:(j + 1) * 1024],
                            x1[:, tau * 1024:(tau + 1) * 1024],
                            stc(S_B3)[:, tau:tau + 1], stc(S_R3)[:, tau:tau + 1],
                            OP.add, OP.mult)
                        vg(vector, ins)
                        ins = vector.tensor_reduce(stc(S_AM)[:, tau:tau + 1],
                                                   A0[:, j * 1024:(j + 1) * 1024],
                                                   AX, OP.max, apply_absolute_value=True)
                        vg(vector, ins)
                        ins = vector.tensor_scalar(stc(S_TMP)[:, tau:tau + 1],
                                                   stc(S_AM)[:, tau:tau + 1],
                                                   1e-12, None, OP.max)
                        vg(vector, ins)
                        ins = vector.reciprocal(stc(S_TMP2)[:, tau:tau + 1],
                                                stc(S_TMP)[:, tau:tau + 1])
                        vg(vector, ins)
                        ins = vector.scalar_tensor_tensor(stc(S_AL)[:, tau:tau + 1],
                                                          stc(S_TMP)[:, tau:tau + 1],
                                                          f32c(1.0 / 448.0), wscal[:, 0:1],
                                                          OP.mult, OP.mult)
                        vg(vector, ins)
                        ins = vector.tensor_scalar(stc(S_TMP)[:, tau:tau + 1],
                                                   stc(S_TMP2)[:, tau:tau + 1],
                                                   f32c(448.0), None, OP.mult)
                        vg(vector, ins)
                        ins = vector.tensor_scalar(A1[:, j * 1024:(j + 1) * 1024],
                                                   A0[:, j * 1024:(j + 1) * 1024],
                                                   stc(S_TMP)[:, tau:tau + 1], None, OP.mult)
                        vg(vector, ins)
                    ins = vector.scalar_tensor_tensor(A2, A1, -1.0, A1, OP.mult, OP.max)
                    vg(vector, ins)
                    ins = vector.tensor_scalar(A0, A2, f32c(2.0 ** -6), None, OP.max)
                    inc(ins, "vtV")
                    vector.wait_ge(sems["rsS"], rs_l2(q))
                    ins = vector.tensor_scalar(E0, C0, f32c(1.0 / LN2F), f32c(191.5),
                                               OP.mult, OP.add)
                    vg(vector, ins)
                    ins = vector.tensor_copy(ints[:, 0:2048], E0)
                    vg(vector, ins)
                    ins = vector.tensor_copy(C0, ints[:, 0:2048])
                    inc(ins, "vtV")
                    vector.wait_ge(sems["rsS"], rs_sx(q))
                    ins = vector.tensor_tensor(A2, A1, C1, OP.mult)
                    vg(vector, ins)
                    ins = vector.tensor_scalar(A1, A2, f32c(1.5 * 2.0 ** 23), None, OP.add)
                    vg(vector, ins)
                    ins = vector.tensor_copy(ints[:, 0:2048], A1)
                    vg(vector, ins)
                    ins = vector.tensor_copy(A1, ints[:, 0:2048])
                    vg(vector, ins)
                    ins = vector.tensor_scalar(A0, A1, f32c(-1.5 * 2.0 ** 23), None, OP.add)
                    vg(vector, ins)
                    ins = vector.tensor_tensor(A1, A2, A0, OP.subtract)
                    vg(vector, ins)
                    ins = vector.tensor_scalar(A2, A1, 0.5, None, OP.is_ge)
                    vg(vector, ins)
                    ins = vector.tensor_scalar(E0, A1, -0.5, None, OP.is_ge)
                    vg(vector, ins)
                    ins = vector.tensor_tensor(A1, A2, E0, OP.add)
                    vg(vector, ins)
                    ins = vector.scalar_tensor_tensor(A2, A1, -1.0, A0, OP.add, OP.add)
                    vg(vector, ins)
                    ins = vector.tensor_tensor(C1, A2, S0, OP.mult)
                    inc(ins, "r4V")
                ins = vector.reciprocal(stc(S_TMP)[:], stc(S_AL)[:])
                vg(vector, ins)
                ins = vector.tensor_copy(ivbf[:], stc(S_TMP)[:])
                inc(ins, "vtV")

            @blk.tensor
            def _(tensor):
                for q in range(NSL if P3_SUB >= 3 else 0):
                    pq = q % 2
                    C1 = bufC[:, pq * 4096 + 2048: pq * 4096 + 4096]
                    tensor.wait_ge(sems["r4V"], qv_t(q))
                    for c in range(NC):
                        r = c % 4
                        if c >= 4:
                            tensor.wait_ge(sems["evS"], ev0 + q * NC + (c - 4) + 1)
                        elif q >= 1:
                            tensor.wait_ge(sems["evS"], ev0 + (q - 1) * NC + (c + 4) + 1)
                        for j in range(2):
                            ins = tensor.matmul(
                                psT[r // 2][:, (r % 2) * 256 + j * 128:(r % 2) * 256 + (j + 1) * 128],
                                C1[:, j * 1024 + c * 128: j * 1024 + (c + 1) * 128],
                                ident[:, 0:128], start=True, stop=True)
                            if j == 1:
                                inc(ins, "trT")
                if P3_SUB == 3:
                    tensor.wait_ge(sems["vtV"], vt_iv)
                    tensor.wait_ge(sems["evS"], ev0 + NSL * NC)
                for g in range(4 if P3_SUB == 3 else 0):
                    pv = g % 2
                    if g >= 2:
                        tensor.wait_ge(sems["evS"], ev0 + NSL * NC + (g - 2) + 1)
                    for j in range(4):
                        tau = 4 * g + j
                        ins = tensor.matmul(psT[pv][0:1, j * 128:(j + 1) * 128],
                                            ivbf[:, tau:tau + 1], ident[:, 0:128],
                                            start=True, stop=True)
                        if j == 3:
                            inc(ins, "trT")

        if PL < 6:
            for cm in cms:
                cm.__exit__(None, None, None)
            return nc
        with nc.Block("p4", no_gpsimd_drain=True) as blk:
            mm0 = cnt["mmT"]
            s20 = cnt["s2S"]
            vt0 = cnt["vtV"]
            rs0 = cnt["rsS"]
            r40 = cnt["r4V"]
            st0 = cnt["st2"]
            ld40 = cnt["ldA"]
            ev40 = cnt["evS"]

            @blk.vector
            def _(vector):
                ins = vector.scalar_tensor_tensor(stc(S_TMP)[:], stc(S_AL)[:], 1.0,
                                                  stc(S_AL)[:], OP.bypass, OP.mult)
                inc(ins, "vtV")
                for tau in range(NT):
                    par = tau % 2
                    vector.wait_ge(sems["mmT"], mm0 + tau + 1)
                    ins = vector.tensor_scalar(stc(S_M2)[:, tau:tau + 1],
                                               psS[:, 2 * tau:2 * tau + 1],
                                               f32c(1.0 / 1024.0), None, OP.mult)
                    vg(vector, ins)
                    ins = vector.scalar_tensor_tensor(stc(S_B2)[:, tau:tau + 1],
                                                      stc(S_M2)[:, tau:tau + 1], 1.0,
                                                      stc(S_M2)[:, tau:tau + 1],
                                                      OP.bypass, OP.mult)
                    vg(vector, ins)
                    vector.wait_ge(sems["s2S"], s20 + tau + 1)
                    ins = vector.scalar_tensor_tensor(stc(S_TMP2)[:, tau:tau + 1],
                                                      stc(S_A2)[:, tau:tau + 1],
                                                      f32c(1.0 / 1024.0),
                                                      stc(S_B2)[:, tau:tau + 1],
                                                      OP.mult, OP.subtract)
                    vg(vector, ins)
                    ins = vector.tensor_tensor(stc(S_V2)[:, tau:tau + 1],
                                               stc(S_TMP2)[:, tau:tau + 1],
                                               stc(S_TMP)[:, tau:tau + 1], OP.mult)
                    inc(ins, "vtV")
                    vector.wait_ge(sems["rsS"], rs0 + tau + 1)
                    ins = vector.tensor_tensor(stc(S_TMP)[:, tau:tau + 1],
                                               stc(S_R2)[:, tau:tau + 1],
                                               stc(S_AL)[:, tau:tau + 1], OP.mult)
                    vg(vector, ins)
                    ins = vector.scalar_tensor_tensor(stc(S_B2)[:, tau:tau + 1],
                                                      stc(S_M2)[:, tau:tau + 1], -1.0,
                                                      stc(S_TMP)[:, tau:tau + 1],
                                                      OP.mult, OP.mult)
                    vg(vector, ins)
                    ins = vector.tensor_single_scalar(x1[:, tau * 1024:(tau + 1) * 1024],
                                                      x1[:, tau * 1024:(tau + 1) * 1024],
                                                      stc(S_B2)[:, tau:tau + 1], OP.add)
                    vg(vector, ins)
                    if tau >= 8:
                        vector.wait_ge(sems["st2"], st0 + 16 * (tau - 7))
                    if tau == 0:
                        vector.wait_ge(sems["evS"], ev40 + 1)
                    oslot = (tau % 8) * 1024
                    ins = vector.scalar_tensor_tensor(bufA[:, oslot:oslot + 1024],
                                                      psA[par][:, 0:1024],
                                                      stc(S_TMP)[:, tau:tau + 1],
                                                      x1[:, tau * 1024:(tau + 1) * 1024],
                                                      OP.mult, OP.add)
                    inc(ins, "r4V")

            @blk.tensor
            def _(tensor):
                tensor.wait_ge(sems["evS"], ev40 + 1)
                for tau in range(NT):
                    par = tau % 2
                    if tau >= 2:
                        tensor.wait_ge(sems["r4V"], r40 + tau - 1)
                        tensor.wait_ge(sems["s2S"], s20 + tau - 1)
                    lrow = onerow[0:1, :] if P4_SUB >= 1 else ivrow[0:1, tau * 128:(tau + 1) * 128]
                    rrow = borow if P4_SUB >= 2 else birow
                    for nh in range(2):
                        for c in range(NC):
                            tensor.matmul(psA[par][:, nh * 512:(nh + 1) * 512],
                                          yq[:, c * TC + tau * 128: c * TC + (tau + 1) * 128],
                                          rhs[:, c * 1026 + nh * 512: c * 1026 + (nh + 1) * 512],
                                          start=(c == 0), stop=False)
                        tensor.matmul(psA[par][:, nh * 512:(nh + 1) * 512],
                                      lrow,
                                      rrow[0:1, nh * 512:(nh + 1) * 512],
                                      start=False, stop=True)
                    for c in range(NC):
                        tensor.matmul(psS[:, 2 * tau:2 * tau + 2],
                                      yq[:, c * TC + tau * 128: c * TC + (tau + 1) * 128],
                                      rhs[:, c * 1026 + 1024: c * 1026 + 1026],
                                      start=(c == 0), stop=False)
                    ins = tensor.matmul(psS[:, 2 * tau:2 * tau + 2],
                                        lrow,
                                        rrow[0:1, 1024:1026],
                                        start=False, stop=True)
                    inc(ins, "mmT")

            @blk.scalar
            def _(scalar):
                scalar.wait_ge(sems["ldA"], ld40 + 16)
                ins = scalar.activation(birow[0:1, 0:1024], bufA[0:1, 0:1024], AF.Copy,
                                        bias=0.0, scale=1.0, accum_out=rowsum[0:1, 1:2])
                sg(scalar, ins)
                ins = scalar.memzero(birow[0:1, 1024:1026])
                sg(scalar, ins)
                ins = scalar.copy(birow[0:1, 1024:1025], rowsum[0:1, 1:2])
                inc(ins, "evS")
                for tau in range(NT):
                    par = tau % 2
                    scalar.wait_ge(sems["mmT"], mm0 + tau + 1)
                    ins = scalar.activation(bufB[:, 0:1024], psA[par][:, 0:1024],
                                            AF.Square, bias=0.0, scale=1.0,
                                            accum_out=stc(S_A2)[:, tau:tau + 1])
                    inc(ins, "s2S")
                    scalar.wait_ge(sems["vtV"], vt0 + 1 + tau + 1)
                    ins = scalar.activation(stc(S_TMP2)[:, tau:tau + 1],
                                            stc(S_V2)[:, tau:tau + 1],
                                            AF.Ln, bias=epsT[:, 0:1], scale=1.0)
                    sg(scalar, ins)
                    ins = scalar.activation(stc(S_R2)[:, tau:tau + 1],
                                            stc(S_TMP2)[:, tau:tau + 1],
                                            AF.Exp, bias=0.0, scale=-0.5)
                    inc(ins, "rsS")

            @blk.sync
            def _(sync):
                sync.dma_start(out=bufA[0:1, 0:1024], in_=bi_d).then_inc(sems["ldA"], 16)
                for tau in range(NT):
                    sync.wait_ge(sems["r4V"], r40 + tau + 1)
                    oslot = (tau % 8) * 1024
                    sync.dma_start(out=out_d[tau * 128:(tau + 1) * 128, :],
                                   in_=bufA[:, oslot:oslot + 1024]).then_inc(sems["st2"], 16)
                sync.wait_ge(sems["st2"], st0 + 16 * NT)
            cnt["ldA"] += 16
            cnt["st2"] += 16 * NT

        for cm in cms:
            cm.__exit__(None, None, None)
    return nc


_NC_CACHE = {}


def kernel(**kw):
    ins = dict(kw)
    xin = np.asarray(ins["L0_attn_x"])
    a = np.ascontiguousarray(np.asarray(ins["L0_attn_a"], dtype=np.float32).reshape(NC, 128).T)
    b = np.ascontiguousarray(np.asarray(ins["L0_attn_b"], dtype=np.float32).reshape(NC, 128).T)
    w_out = np.asarray(ins["L0_attn_w_out"], dtype=np.float32)
    b_out = np.ascontiguousarray(np.asarray(ins["L0_attn_b_out"], dtype=np.float32).reshape(1, D))
    w8 = np.asarray(ins["L0_ffn_weight"])
    ws = np.float32(np.asarray(ins["L0_ffn_weight_scale"]))
    bias = np.ascontiguousarray(np.asarray(ins["L0_ffn_bias"], dtype=np.float32).reshape(1, D))

    wt = np.ascontiguousarray(w_out.T)
    wf = np.ascontiguousarray(w8.T).astype(ml_dtypes.bfloat16)
    ws_rep = np.full((128, 1), ws, dtype=np.float32)
    ident = np.concatenate([np.eye(128, dtype=ml_dtypes.bfloat16),
                            -np.eye(128, dtype=ml_dtypes.bfloat16)], axis=1)
    ident = np.ascontiguousarray(ident)

    if "prog" not in _NC_CACHE:
        _NC_CACHE["prog"] = build_program()
    nc = _NC_CACHE["prog"]

    in_maps = []
    for core in range(8):
        bb, hh = core // 2, core % 2
        xo = np.ascontiguousarray(xin[bb, hh * TC:(hh + 1) * TC].astype(np.float32, copy=False))
        xp = np.ascontiguousarray(xin[bb, 0:TC].astype(np.float32, copy=False)) if hh == 1 else xo
        hs = np.full((128, 1), np.float32(hh))
        in_maps.append({
            "x": xo, "xp": np.array(xp), "wt": np.array(wt), "wf": np.array(wf),
            "av": np.array(a), "bv": np.array(b), "bo": np.array(b_out),
            "bi": np.array(bias), "wsr": np.array(ws_rep), "ident": np.array(ident),
            "hs": hs,
        })
    res = None
    for attempt, tr in enumerate((True, False, False)):
        try:
            res = run_bass_kernel_spmd(nc, in_maps, core_ids=list(range(8)),
                                       trace=tr, trace_cores=[0] if tr else None)
            break
        except RuntimeError as e:
            sys.stderr.write(f"[kernel] attempt {attempt}: {type(e).__name__}: {e}\n")
            if "_nrt_profile" not in str(e):
                raise
    if res is None:
        raise RuntimeError("kernel: all attempts failed")
    out = np.empty((B, T, D), dtype=np.float32)
    for core in range(8):
        bb, hh = core // 2, core % 2
        out[bb, hh * TC:(hh + 1) * TC] = res.results[core]["out"]
    return out
